# version 02
# baseline (speedup 1.0000x reference)
import math
from contextlib import ExitStack

import numpy as np

import concourse.bass as bass
import concourse.mybir as mybir
from concourse.bass_utils import run_bass_kernel_spmd

F32 = mybir.dt.float32
BF16 = mybir.dt.bfloat16
I32 = mybir.dt.int32
AF = mybir.ActivationFunctionType
ALU = mybir.AluOpType
AX = mybir.AxisListType

B_, T_, E_ = 4, 4096, 4096
H_, HKV, DH = 32, 8, 128
FFN = 11008
NI = 86
KC = 32
NQT = 16
OWN = 2048
VW = 1032
ENGINES = ("sync", "gpsimd", "tensor", "vector", "scalar")
NR_GU = 1376
NR_D = 1024


def _rope_tables():
    ROPE_THETA, ROPE_FACTOR = 500000.0, 32.0
    HI, LO, ORIG = 4.0, 1.0, 8192
    inv = 1.0 / ROPE_THETA ** (np.arange(0, DH, 2, dtype=np.float32) / DH)
    wav = (2.0 * math.pi / inv).astype(np.float32)
    cut_lo, cut_hi = ORIG / HI, ORIG / LO
    inv = np.where(wav > cut_hi, inv / ROPE_FACTOR, inv).astype(np.float32)
    smooth = ((ORIG / wav - LO) / (HI - LO)).astype(np.float32)
    smoothed = (1.0 - smooth) * inv / ROPE_FACTOR + smooth * inv
    med = ~(wav < cut_hi) & ~(wav > cut_lo)
    inv = np.where(med, smoothed, inv).astype(np.float32)
    pos = np.arange(T_, dtype=np.float32)
    fr = pos[:, None] * inv[None, :]
    return np.cos(fr).astype(np.float32), np.sin(fr).astype(np.float32)


def _host_tables(r):
    import ml_dtypes
    cos, sin = _rope_tables()
    own_pos = ((2 * np.arange(NQT)[:, None] + r) * 128
               + np.arange(128)[None, :]).reshape(-1)
    sc = np.float32(1.0 / math.sqrt(DH))
    cq = np.empty((128, OWN), np.float32)
    sq = np.empty((128, OWN), np.float32)
    ck = np.empty((128, T_), np.float32)
    sk = np.empty((128, T_), np.float32)
    for p in range(128):
        f = p % 64
        sgn = -1.0 if p < 64 else 1.0
        cq[p] = cos[own_pos, f] * sc
        sq[p] = sin[own_pos, f] * (sc * sgn)
        ck[p] = cos[:, f]
        sk[p] = sin[:, f] * sgn
    tri = (np.arange(128)[:, None] <= np.arange(128)[None, :])
    m = np.zeros((128, 256), np.float32)
    if r == 0:
        m[:, 0:128] = tri
    else:
        m[:, 0:128] = 1.0
        m[:, 128:256] = tri
    bf = ml_dtypes.bfloat16
    ind_gu = np.zeros((128, 63), np.float32); ind_gu[:, 31] = 1.0
    ind_d = np.zeros((128, 171), np.float32); ind_d[:, 85] = 1.0
    return dict(cosq=cq, sinq=sq, cosk=ck, sink=sk,
                maskT=m.astype(bf),
                ident_bf=np.eye(128, dtype=bf),
                ident_f=np.eye(128, dtype=np.float32),
                ones_row=np.ones((1, 128), np.float32),
                ones_col_f=np.ones((128, 1), np.float32),
                ones_col_b=np.ones((128, 1), dtype=bf),
                ind_gu=ind_gu.astype(bf), ind_d=ind_d.astype(bf))


class Ctr:
    __slots__ = ("h", "v")

    def __init__(self, h):
        self.h = h
        self.v = 0


class Plan:
    __slots__ = ("items",)

    def __init__(self):
        self.items = []

    def add(self, eng, fn=None, waits=(), inc=None):
        ws = tuple((w[0].h if isinstance(w[0], Ctr) else w[0], w[1])
                   for w in waits)
        if inc is not None:
            ctr, amt = inc
            ctr.v += amt
            inc = (ctr.h, amt)
        self.items.append((eng, ws, fn, inc))


def emit_block(nc, plan):
    with nc.Block() as blk:
        for name in ENGINES:
            items = [it for it in plan.items if it[0] == name]
            if not items:
                continue

            def make(items=items):
                def body(eng):
                    seen = {}
                    for _, waits, fn, inc in items:
                        for (h, v) in waits:
                            k = id(h)
                            if v > 0 and seen.get(k, -1) < v:
                                eng.wait_ge(h, v)
                                seen[k] = v
                        inst = fn(eng) if fn is not None else None
                        if inc is not None:
                            h, amt = inc
                            if inst is None:
                                inst = eng.nop()
                            inst.then_inc(h, amt)
                return body

            getattr(blk, name)(make())


DEBUG_TAPS = [False]
DBG = {"norope": False, "serial1": False, "nodq1": False, "p1only": False,
       "nostats1": False}


def build_nc():
    nc = bass.Bass("TRN2", target_bir_lowering=False)
    skind = "ExternalOutput" if DEBUG_TAPS[0] else "Internal"

    def din(name, shape, dt):
        return nc.dram_tensor(name, shape, dt, kind="ExternalInput").ap()

    xT = din("xT", [E_, T_], F32)
    xTo = din("xTown", [E_, OWN], F32)
    wqT = din("wqT", [E_, E_], F32)
    wkT = din("wkT", [E_, 1024], F32)
    wvT = din("wvT", [E_, 1024], F32)
    woT = din("woT", [E_, E_], F32)
    gq = din("gq", [E_, FFN], I32)
    gs = din("gs", [32, FFN], F32)
    gz = din("gz", [32, FFN], F32)
    uq = din("uq", [E_, FFN], I32)
    us = din("us", [32, FFN], F32)
    uz = din("uz", [32, FFN], F32)
    dqm = din("dq", [FFN, E_], I32)
    dsm = din("ds", [NI, E_], F32)
    dzm = din("dz", [NI, E_], F32)
    cosq = din("cosq", [128, OWN], F32)
    sinq = din("sinq", [128, OWN], F32)
    cosk = din("cosk", [128, T_], F32)
    sink = din("sink", [128, T_], F32)
    maskT = din("maskT", [128, 256], BF16)
    identb = din("ident_bf", [128, 128], BF16)
    identf = din("ident_f", [128, 128], F32)
    onesr = din("ones_row", [1, 128], F32)
    onescf = din("ones_col_f", [128, 1], F32)
    onescb = din("ones_col_b", [128, 1], BF16)
    indgu = din("ind_gu", [128, 63], BF16)
    indd = din("ind_d", [128, 171], BF16)

    out = nc.dram_tensor("out", [OWN, E_], F32, kind="ExternalOutput").ap()

    q_scr = nc.dram_tensor("q_scr", [E_, OWN], BF16, kind=skind).ap()
    if DEBUG_TAPS[0]:
        ktap = nc.dram_tensor("ktap", [128, HKV * T_], BF16,
                              kind="ExternalOutput").ap()
        vtap = nc.dram_tensor("vtap", [128, 32 * VW], BF16,
                              kind="ExternalOutput").ap()
        ptap = nc.dram_tensor("ptap", [128, T_], BF16,
                              kind="ExternalOutput").ap()
        btap = nc.dram_tensor("btap", [128, 32], F32,
                              kind="ExternalOutput").ap()
    ctx_scr = nc.dram_tensor("ctx_scr", [E_, OWN], BF16, kind=skind).ap()
    h1_scr = nc.dram_tensor("h1_scr", [E_, OWN], BF16, kind=skind).ap()
    wg_scr = nc.dram_tensor("wg_scr", [E_ + 128, FFN], BF16, kind=skind).ap()
    wu_scr = nc.dram_tensor("wu_scr", [E_ + 128, FFN], BF16).ap()
    wd_scr = nc.dram_tensor("wd_scr", [FFN + 128, E_], BF16, kind=skind).ap()
    pre_scr = nc.dram_tensor("pre_scr", [E_, OWN], F32, kind=skind).ap()

    with ExitStack() as g:
        eg = g.enter_context
        LD = Ctr(eg(nc.semaphore("LD")))
        LS = Ctr(eg(nc.semaphore("LS")))
        LA = Ctr(eg(nc.semaphore("LA")))
        PE = Ctr(eg(nc.semaphore("PE")))
        VE = Ctr(eg(nc.semaphore("VE")))
        SE = Ctr(eg(nc.semaphore("SE")))
        WB = Ctr(eg(nc.semaphore("WB")))
        OUTC = Ctr(eg(nc.semaphore("OUTC")))
        BAR = Ctr(eg(nc.semaphore("BAR")))
        DQL = Ctr(eg(nc.semaphore("DQL")))
        DQR = Ctr(eg(nc.semaphore("DQR")))
        DQV = Ctr(eg(nc.semaphore("DQV")))
        DQW = Ctr(eg(nc.semaphore("DQW")))

        ident_bf = eg(nc.sbuf_tensor("g_identb", [128, 128], BF16))
        ident_f = eg(nc.sbuf_tensor("g_identf", [128, 128], F32))
        ones_row = eg(nc.sbuf_tensor("g_onesr", [1, 128], F32))
        ones_col_f = eg(nc.sbuf_tensor("g_onescf", [128, 1], F32))
        ones_col_b = eg(nc.sbuf_tensor("g_onescb", [128, 1], BF16))
        mask_sb = eg(nc.sbuf_tensor("g_mask", [128, 256], BF16))
        ind_gu = eg(nc.sbuf_tensor("g_indgu", [128, 63], BF16))
        ind_d = eg(nc.sbuf_tensor("g_indd", [128, 171], BF16))
        negb_sb = eg(nc.sbuf_tensor("g_negb", [128, 32], F32))
        maxq2 = eg(nc.sbuf_tensor("g_maxq2", [1, 32], F32))
        maxk2 = eg(nc.sbuf_tensor("g_maxk2", [1, 8], F32))
        brow = eg(nc.sbuf_tensor("g_brow", [1, 32], F32))

        DQ_ITEMS = ([("g", kc, ri) for kc in range(KC) for ri in range(8)]
                    + [("u", kc, ri) for kc in range(KC) for ri in range(8)]
                    + [("d", kc, ri) for kc in range(NI) for ri in range(4)])
        DQW_ALL = 16 * len(DQ_ITEMS)
        dqs = {"i": 0, "marks": {}}

        def emit_dq(plan, bufs, n_items):
            qb, sb, ob = bufs
            for _ in range(n_items):
                i = dqs["i"]
                if i >= len(DQ_ITEMS):
                    return
                dqs["i"] = i + 1
                mat, kc, ri = DQ_ITEMS[i]
                if mat == "g":
                    qsrc, ssrc, dst, nr = gq, gs, wg_scr, NR_GU
                elif mat == "u":
                    qsrc, ssrc, dst, nr = uq, us, wu_scr, NR_GU
                else:
                    qsrc, ssrc, dst, nr = dqm, dsm, wd_scr, NR_D
                ncols = qsrc.shape[1]
                col0 = ri * nr
                bi = i % 2
                prev = dqs["marks"].get(i - 2)
                w = [(DQV, prev[0])] if prev else []
                plan.add("sync",
                         (lambda eng, d=qb[bi], s=qsrc, kc=kc, c0=col0, nr=nr,
                          ncols=ncols: eng.dma_start(
                              d[:, 0:nr],
                              bass.AP(s.tensor, kc * 128 * ncols + c0,
                                      [[ncols, 128], [1, nr]]))),
                         waits=w, inc=(DQL, 16))
                dql = DQL.v
                plan.add("gpsimd",
                         (lambda eng, d=sb[bi], s=ssrc, kc=kc, c0=col0, nr=nr,
                          ncols=ncols: eng.dma_start(
                              d[:, 0:nr],
                              bass.AP(s.tensor, kc * ncols + c0,
                                      [[0, 128], [1, nr]]),
                              allow_slow_non_contiguous=True)),
                         waits=list(w), inc=(DQR, 16))
                dqr = DQR.v
                w2 = [(DQL, dql), (DQR, dqr)]
                if prev:
                    w2.append((DQW, prev[1]))
                plan.add("vector",
                         (lambda eng, q=qb[bi], s=sb[bi], o=ob[bi], nr=nr:
                          eng.scalar_tensor_tensor(o[:, 0:nr], q[:, 0:nr],
                                                   8.0, s[:, 0:nr],
                                                   ALU.subtract, ALU.mult)),
                         waits=w2, inc=(DQV, 1))
                dqv = DQV.v
                plan.add("sync",
                         (lambda eng, o=ob[bi], d=dst, kc=kc, c0=col0, nr=nr,
                          ncols=ncols: eng.dma_start(
                              bass.AP(d.tensor, kc * 128 * ncols + c0,
                                      [[ncols, 128], [1, nr]]),
                              o[:, 0:nr])),
                         waits=[(DQV, dqv)], inc=(DQW, 16))
                dqs["marks"][i] = (dqv, DQW.v)

        def mk_dq_bufs(ctx, tag):
            e = ctx.enter_context
            qb = [e(nc.sbuf_tensor(f"{tag}_dqq{i}", [128, NR_GU], I32))
                  for i in range(2)]
            sb = [e(nc.sbuf_tensor(f"{tag}_dqs{i}", [128, NR_GU], BF16))
                  for i in range(2)]
            ob = [e(nc.sbuf_tensor(f"{tag}_dqo{i}", [128, NR_GU], BF16))
                  for i in range(2)]
            return (qb, sb, ob)

        def barrier(plan):
            for en in ENGINES:
                plan.add(en, None, inc=(BAR, 1))
            return BAR.v

        def start_phase(plan, bar_val):
            for en in ENGINES:
                plan.add(en, None, waits=[(BAR, bar_val)])

        p0 = Plan()
        for d, s in ((ident_bf, identb), (ident_f, identf), (ones_row, onesr),
                     (ones_col_f, onescf), (ones_col_b, onescb),
                     (mask_sb, maskT), (ind_gu, indgu), (ind_d, indd)):
            p0.add("sync", (lambda eng, d=d, s=s: eng.dma_start(d[:], s)),
                   inc=(LS, 16))
        ld_consts = LS.v
        with nc.sbuf_tensor("p0_zb", [NI, FFN], BF16) as zb:
            for i, (dst, src, r0) in enumerate(
                    ((wg_scr, gz, E_), (wu_scr, uz, E_), (wd_scr, dzm, FFN))):
                nr = src.shape[0]
                p0.add("gpsimd",
                       (lambda eng, src=src, nr=nr: eng.dma_start(
                           zb[0:nr, 0:src.shape[1]], src)),
                       waits=[(WB, WB.v)], inc=(DQR, 16))
                zl = DQR.v
                p0.add("sync",
                       (lambda eng, dst=dst, r0=r0, nr=nr, src=src:
                        eng.dma_start(dst[r0:r0 + nr, :],
                                      zb[0:nr, 0:src.shape[1]])),
                       waits=[(DQR, zl)], inc=(WB, 16))
            p0.add("vector", lambda eng: eng.memset(maxq2[:], -3.0e38))
            p0.add("vector", lambda eng: eng.memset(maxk2[:], -3.0e38),
                   inc=(VE, 1))
            for en in ("vector", "scalar", "tensor"):
                p0.add(en, None, waits=[(LS, ld_consts)])
            for en in ("sync", "gpsimd"):
                p0.add(en, None, waits=[(WB, WB.v), (LS, ld_consts)])
            bar0 = barrier(p0)
            emit_block(nc, p0)

        with ExitStack() as s1:
            e1 = s1.enter_context
            x_own = e1(nc.sbuf_tensor("p1_x", [128, KC * 1024], BF16))
            wqb = [e1(nc.sbuf_tensor(f"p1_wq{i}", [128, KC * 128], BF16))
                   for i in range(2)]
            cq_sb = e1(nc.sbuf_tensor("p1_cq", [128, 1024], F32))
            sq_sb = e1(nc.sbuf_tensor("p1_sq", [128, 1024], F32))
            t1 = e1(nc.sbuf_tensor("p1_t1", [128, 512], F32))
            t2 = e1(nc.sbuf_tensor("p1_t2", [128, 512], F32))
            qbf = [e1(nc.sbuf_tensor(f"p1_qb{i}", [128, 1024], BF16))
                   for i in range(2)]
            q2b = e1(nc.sbuf_tensor("p1_q2", [128, 512], BF16))
            mtmp = e1(nc.sbuf_tensor("p1_mt", [1, 2], F32))
            dqb1 = mk_dq_bufs(s1, "p1")
            q_ps = [e1(nc.psum_tensor(f"p1_qps{i}", [128, 512], F32))
                    for i in range(2)]
            st_ps = [e1(nc.psum_tensor(f"p1_st{i}", [1, 512], F32))
                     for i in range(2)]

            p1 = Plan()
            start_phase(p1, bar0)
            pe_blk = {}
            ve_rope = {}
            ve_rmax = {}
            wb_q = {}
            gi = 0
            for th in range(2):
                p1.add("sync",
                       (lambda eng, th=th: eng.dma_start(
                           cq_sb[:], cosq[:, th * 1024:(th + 1) * 1024])),
                       waits=([(PE, pe_blk.get(KC - 1, 0)),
                               (VE, ve_rope.get(2 * KC - 1, 0))]
                              if th == 1 else []),
                       inc=(LS, 16))
                p1.add("sync",
                       (lambda eng, th=th: eng.dma_start(
                           sq_sb[:], sinq[:, th * 1024:(th + 1) * 1024])),
                       inc=(LS, 16))
                ld_tab = LS.v
                for cg in range(0, KC, 8):
                    p1.add("gpsimd",
                           (lambda eng, th=th, cg=cg: eng.dma_start(
                               bass.AP(x_own, cg * 1024,
                                       [[KC * 1024, 128], [1024, 8],
                                        [1, 1024]]),
                               bass.AP(xTo.tensor,
                                       cg * 128 * OWN + th * 1024,
                                       [[OWN, 128], [128 * OWN, 8],
                                        [1, 1024]]))),
                           waits=([(PE, pe_blk.get(KC - 1, 0))]
                                  if th == 1 and cg == 0 else []),
                           inc=(LD, 16))
                ld_x = LD.v
                for f in range(H_):
                    it = th * H_ + f
                    w = []
                    if it >= 2:
                        w.append((PE, pe_blk[it - 2]))
                    for lo in (0, 16):
                        p1.add("gpsimd",
                               (lambda eng, f=f, lo=lo, it=it: eng.dma_start(
                                   bass.AP(wqb[it % 2], lo * 128,
                                           [[KC * 128, 128], [128, 16],
                                            [1, 128]]),
                                   bass.AP(wqT.tensor,
                                           lo * 128 * E_ + f * 128,
                                           [[E_, 128], [128 * E_, 16],
                                            [1, 128]]))),
                               waits=(list(w) if lo == 0 else []),
                               inc=(LD, 16))
                    ld_wq = LD.v
                    for half in range(2):
                        for ec in range(KC):
                            waits = []
                            if ec == 0:
                                waits = [(LD, max(ld_wq, ld_x))]
                                if gi >= 2:
                                    waits.append((VE, ve_rope[gi - 2]))
                            inc = (PE, 1) if ec == KC - 1 else None
                            p1.add("tensor",
                                   (lambda eng, it=it, ec=ec, half=half,
                                    gi=gi: eng.matmul(
                                        q_ps[gi % 2][:],
                                        wqb[it % 2][:, ec * 128:
                                                    ec * 128 + 128],
                                        x_own[:, ec * 1024 + half * 512:
                                              ec * 1024 + half * 512 + 512],
                                        start=(ec == 0),
                                        stop=(ec == KC - 1))),
                                   waits=waits, inc=inc)
                        pe_mm = PE.v
                        colb = half * 512
                        w2 = [(PE, pe_mm), (LS, ld_tab)]
                        if gi >= 2:
                            w2.append((WB, wb_q.get(it - 2, 0)))
                        if DBG["norope"]:
                            p1.add("vector",
                                   (lambda eng, gi=gi, it=it, colb=colb:
                                    eng.tensor_copy(
                                        qbf[it % 2][:, colb:colb + 512],
                                        q_ps[gi % 2][:])),
                                   waits=w2, inc=(VE, 1))
                            ve_rope[gi] = VE.v
                        else:
                            p1.add("vector",
                                   (lambda eng, gi=gi, colb=colb:
                                    eng.tensor_mul(t1[:], q_ps[gi % 2][:],
                                                   cq_sb[:,
                                                         colb:colb + 512])),
                                   waits=w2)
                            p1.add("vector",
                                   (lambda eng, gi=gi, colb=colb:
                                    eng.tensor_mul(t2[0:64, :],
                                                   q_ps[gi % 2][64:128, :],
                                                   sq_sb[0:64,
                                                         colb:colb + 512])))
                            p1.add("vector",
                                   (lambda eng, gi=gi, colb=colb:
                                    eng.tensor_mul(t2[64:128, :],
                                                   q_ps[gi % 2][0:64, :],
                                                   sq_sb[64:128,
                                                         colb:colb + 512])))
                            p1.add("vector",
                                   (lambda eng, it=it, colb=colb:
                                    eng.tensor_add(
                                        qbf[it % 2][:, colb:colb + 512],
                                        t1[:], t2[:])),
                                   inc=(VE, 1))
                            ve_rope[gi] = VE.v
                        if DBG["nostats1"]:
                            p1.add("tensor", None, inc=(PE, 1))
                            p1.add("vector", None, inc=(VE, 2))
                        else:
                            p1.add("vector",
                                   (lambda eng, it=it, colb=colb:
                                    eng.tensor_mul(
                                        q2b[:],
                                        qbf[it % 2][:, colb:colb + 512],
                                        qbf[it % 2][:, colb:colb + 512])),
                                   inc=(VE, 1))
                            ve_q2 = VE.v
                            p1.add("tensor",
                                   (lambda eng, gi=gi: eng.matmul(
                                       st_ps[gi % 2][0:1, :], ones_col_b[:],
                                       q2b[:], start=True, stop=True)),
                                   waits=[(VE, ve_q2)], inc=(PE, 1))
                            pe_st = PE.v
                            p1.add("vector",
                                   (lambda eng, gi=gi:
                                    eng.reduce_max(mtmp[0:1, 0:1],
                                                   st_ps[gi % 2][0:1, :],
                                                   axis=AX.X)),
                                   waits=[(PE, pe_st)])
                            p1.add("vector",
                                   (lambda eng, f=f:
                                    eng.tensor_max(maxq2[0:1, f:f + 1],
                                                   maxq2[0:1, f:f + 1],
                                                   mtmp[0:1, 0:1])),
                                   inc=(VE, 1))
                        ve_rmax[gi] = VE.v
                        gi += 1
                    pe_blk[it] = PE.v
                    p1.add("sync",
                           (lambda eng, f=f, th=th, it=it: eng.dma_start(
                               q_scr[f * 128:(f + 1) * 128,
                                     th * 1024:(th + 1) * 1024],
                               qbf[it % 2][:])),
                           waits=[(VE, ve_rope[gi - 1])], inc=(WB, 16))
                    wb_q[it] = WB.v
                    if DBG["serial1"]:
                        bv = barrier(p1)
                        start_phase(p1, bv)
                        p1.add("sync", None, waits=[(WB, WB.v)])
                        p1.add("gpsimd", None, waits=[(LD, LD.v)])
                    if not DBG["nodq1"]:
                        emit_dq(p1, dqb1, 4)
            wb_q_all = WB.v
            if DBG["p1only"]:
                p1.add("sync", None, waits=[(WB, WB.v)])
                emit_block(nc, p1)
                return nc
            bar1 = barrier(p1)
            emit_block(nc, p1)

        with (nc.sbuf_tensor("kv_K", [128, HKV * T_], BF16) as K_sb,
              nc.sbuf_tensor("kv_V", [128, 32 * VW], BF16) as V_sb):

            with ExitStack() as s2:
                e2 = s2.enter_context
                x_e = e2(nc.sbuf_tensor("p2_x", [128, KC * 512], BF16))
                wkv = [e2(nc.sbuf_tensor(f"p2_w{i}", [128, KC * 128], BF16))
                       for i in range(2)]
                ck_sb = e2(nc.sbuf_tensor("p2_ck", [128, 512], F32))
                sk_sb = e2(nc.sbuf_tensor("p2_sk", [128, 512], F32))
                t1b = e2(nc.sbuf_tensor("p2_t1", [128, 512], F32))
                t2b = e2(nc.sbuf_tensor("p2_t2", [128, 512], F32))
                k2b = e2(nc.sbuf_tensor("p2_k2", [128, 512], BF16))
                vtmp = e2(nc.sbuf_tensor("p2_vt", [128, 512], BF16))
                mtmp2 = e2(nc.sbuf_tensor("p2_mt", [1, 2], F32))
                kv_ps = [e2(nc.psum_tensor(f"p2_kv{i}", [128, 512], F32))
                         for i in range(2)]
                st2_ps = [e2(nc.psum_tensor(f"p2_st{i}", [1, 512], F32))
                          for i in range(2)]
                tr_ps = e2(nc.psum_tensor("p2_tr", [128, 512], BF16))
                nb_ps = e2(nc.psum_tensor("p2_nb", [128, 128], F32))

                p2 = Plan()
                start_phase(p2, bar1)
                p2.add("vector",
                       (lambda eng: eng.memset(
                           bass.AP(V_sb, 128,
                                   [[32 * VW, 128], [VW, 32], [129, 8],
                                    [1, 1]]),
                           1.0)),
                       inc=(VE, 1))
                ve_ones = VE.v
                pe_blk2 = {}
                ve_done = {}
                ve_trev = {}
                gi = 0
                for q8 in range(8):
                    wtab = ([(VE, ve_done.get((q8 - 1) * 16 + 7, 0))]
                            if q8 > 0 else [])
                    p2.add("sync",
                           (lambda eng, q8=q8: eng.dma_start(
                               ck_sb[:], cosk[:, q8 * 512:(q8 + 1) * 512])),
                           waits=wtab, inc=(LS, 16))
                    p2.add("sync",
                           (lambda eng, q8=q8: eng.dma_start(
                               sk_sb[:], sink[:, q8 * 512:(q8 + 1) * 512])),
                           inc=(LS, 16))
                    ld_tab2 = LS.v
                    for cg in range(0, KC, 16):
                        p2.add("gpsimd",
                               (lambda eng, q8=q8, cg=cg: eng.dma_start(
                                   bass.AP(x_e, cg * 512,
                                           [[KC * 512, 128], [512, 16],
                                            [1, 512]]),
                                   bass.AP(xT.tensor,
                                           cg * 128 * T_ + q8 * 512,
                                           [[T_, 128], [128 * T_, 16],
                                            [1, 512]]))),
                               waits=([(PE, pe_blk2.get(gi - 1, 0))]
                                      if cg == 0 and q8 > 0 else []),
                               inc=(LD, 16))
                    ld_xe = LD.v
                    for fb in range(16):
                        isk = fb < 8
                        fk = fb % 8
                        wsrc = wkT if isk else wvT
                        w = []
                        if gi >= 2:
                            w.append((PE, pe_blk2[gi - 2]))
                        for lo in (0, 16):
                            p2.add("gpsimd",
                                   (lambda eng, gi=gi, wsrc=wsrc, fk=fk,
                                    lo=lo: eng.dma_start(
                                        bass.AP(wkv[gi % 2], lo * 128,
                                                [[KC * 128, 128], [128, 16],
                                                 [1, 128]]),
                                        bass.AP(wsrc.tensor,
                                                lo * 128 * 1024 + fk * 128,
                                                [[1024, 128],
                                                 [128 * 1024, 16],
                                                 [1, 128]]))),
                                   waits=(list(w) if lo == 0 else []),
                                   inc=(LD, 16))
                        ld_w = LD.v
                        for ec in range(KC):
                            waits = []
                            if ec == 0:
                                waits = [(LD, max(ld_w, ld_xe))]
                                if gi >= 2:
                                    waits.append((VE, ve_done[gi - 2]))
                            inc = (PE, 1) if ec == KC - 1 else None
                            p2.add("tensor",
                                   (lambda eng, gi=gi, ec=ec: eng.matmul(
                                       kv_ps[gi % 2][:],
                                       wkv[gi % 2][:, ec * 128:
                                                   ec * 128 + 128],
                                       x_e[:, ec * 512:ec * 512 + 512],
                                       start=(ec == 0),
                                       stop=(ec == KC - 1))),
                                   waits=waits, inc=inc)
                        pe_mm = PE.v
                        if isk:
                            dcol = fk * T_ + q8 * 512
                            w2 = [(PE, pe_mm), (LS, ld_tab2)]
                            p2.add("vector",
                                   (lambda eng, gi=gi:
                                    eng.tensor_mul(t1b[:], kv_ps[gi % 2][:],
                                                   ck_sb[:])),
                                   waits=w2)
                            p2.add("vector",
                                   (lambda eng, gi=gi:
                                    eng.tensor_mul(t2b[0:64, :],
                                                   kv_ps[gi % 2][64:128, :],
                                                   sk_sb[0:64, :])))
                            p2.add("vector",
                                   (lambda eng, gi=gi:
                                    eng.tensor_mul(t2b[64:128, :],
                                                   kv_ps[gi % 2][0:64, :],
                                                   sk_sb[64:128, :])))
                            p2.add("vector",
                                   (lambda eng, dcol=dcol:
                                    eng.tensor_add(K_sb[:, dcol:dcol + 512],
                                                   t1b[:], t2b[:])),
                                   inc=(VE, 1))
                            ve_done[gi] = VE.v
                            p2.add("vector",
                                   (lambda eng, dcol=dcol:
                                    eng.tensor_mul(k2b[:],
                                                   K_sb[:, dcol:dcol + 512],
                                                   K_sb[:,
                                                        dcol:dcol + 512])),
                                   inc=(VE, 1))
                            ve_k2 = VE.v
                            p2.add("tensor",
                                   (lambda eng, gi=gi: eng.matmul(
                                       st2_ps[gi % 2][0:1, :],
                                       ones_col_b[:], k2b[:],
                                       start=True, stop=True)),
                                   waits=[(VE, ve_k2)], inc=(PE, 1))
                            pe_st = PE.v
                            p2.add("vector",
                                   (lambda eng, gi=gi:
                                    eng.reduce_max(mtmp2[0:1, 0:1],
                                                   st2_ps[gi % 2][0:1, :],
                                                   axis=AX.X)),
                                   waits=[(PE, pe_st)])
                            p2.add("vector",
                                   (lambda eng, fk=fk:
                                    eng.tensor_max(maxk2[0:1, fk:fk + 1],
                                                   maxk2[0:1, fk:fk + 1],
                                                   mtmp2[0:1, 0:1])),
                                   inc=(VE, 1))
                            ve_done[gi] = VE.v
                        else:
                            w2 = [(PE, pe_mm), (VE, ve_ones)]
                            prev_tr = ve_trev.get(gi - 1, 0)
                            p2.add("vector",
                                   (lambda eng, gi=gi:
                                    eng.tensor_copy(vtmp[:],
                                                    kv_ps[gi % 2][:])),
                                   waits=w2, inc=(VE, 1))
                            ve_v = VE.v
                            ve_done[gi] = ve_v
                            for tt in range(4):
                                wtr = []
                                if tt == 0:
                                    wtr = [(VE, ve_v), (VE, prev_tr)]
                                p2.add("tensor",
                                       (lambda eng, tt=tt: eng.transpose(
                                           tr_ps[:, tt * 128:tt * 128 + 128],
                                           vtmp[:, tt * 128:tt * 128 + 128],
                                           ident_bf[:])),
                                       waits=wtr,
                                       inc=((PE, 1) if tt == 3 else None))
                            petr = PE.v
                            p2.add("vector",
                                   (lambda eng, q8=q8, fk=fk:
                                    eng.tensor_copy(
                                        bass.AP(V_sb,
                                                (q8 * 4) * VW + fk * 129,
                                                [[32 * VW, 128], [VW, 4],
                                                 [1, 128]]),
                                        tr_ps[:, 0:512])),
                                   waits=[(PE, petr)], inc=(VE, 1))
                            ve_trev[gi] = VE.v
                            ve_done[gi] = VE.v
                        pe_blk2[gi] = PE.v
                        gi += 1
                p2.add("vector",
                       (lambda eng: eng.tensor_tensor(
                           bass.AP(brow, 0, [[32, 1], [4, 8], [1, 4]]),
                           bass.AP(maxq2, 0, [[32, 1], [4, 8], [1, 4]]),
                           bass.AP(maxk2, 0, [[8, 1], [1, 8], [0, 4]]),
                           ALU.mult)),
                       waits=[(VE, VE.v)], inc=(VE, 1))
                ve_b = VE.v
                p2.add("scalar",
                       (lambda eng: eng.sqrt(brow[0:1, :], brow[0:1, :])),
                       waits=[(VE, ve_b)], inc=(SE, 1))
                se_b = SE.v
                p2.add("vector",
                       (lambda eng: eng.tensor_scalar(
                           brow[0:1, :], brow[0:1, :], -1.02, -1.0,
                           ALU.mult, ALU.add)),
                       waits=[(SE, se_b)], inc=(VE, 1))
                ve_b2 = VE.v
                p2.add("tensor",
                       (lambda eng: eng.matmul(nb_ps[:, 0:32], ones_row[:],
                                               brow[0:1, :], start=True,
                                               stop=True)),
                       waits=[(VE, ve_b2)], inc=(PE, 1))
                pe_b = PE.v
                p2.add("vector",
                       (lambda eng: eng.tensor_copy(negb_sb[:],
                                                    nb_ps[:, 0:32])),
                       waits=[(PE, pe_b)], inc=(VE, 1))
                if DEBUG_TAPS[0]:
                    p2.add("sync",
                           (lambda eng: eng.dma_start(ktap, K_sb[:])),
                           waits=[(VE, VE.v), (PE, PE.v)], inc=(WB, 16))
                    p2.add("sync",
                           (lambda eng: eng.dma_start(vtap, V_sb[:])),
                           inc=(WB, 16))
                    p2.add("sync",
                           (lambda eng: eng.dma_start(btap, negb_sb[:])),
                           inc=(WB, 16))
                    p2.add("sync", None, waits=[(WB, WB.v)])
                bar2 = barrier(p2)
                emit_block(nc, p2)

            with ExitStack() as s3:
                e3 = s3.enter_context
                q_sb = e3(nc.sbuf_tensor("p3_q", [128, H_ * 128], BF16))
                pT = [e3(nc.sbuf_tensor(f"p3_p{i}", [128, T_], BF16))
                      for i in range(2)]
                ctm = [e3(nc.sbuf_tensor(f"p3_c{i}", [128, 128], BF16))
                       for i in range(2)]
                rcol = e3(nc.sbuf_tensor("p3_r", [128, 1], F32))
                cfm = e3(nc.sbuf_tensor("p3_f", [128, H_ * 128], BF16))
                dqb3 = mk_dq_bufs(s3, "p3")
                sT_ps = [e3(nc.psum_tensor(f"p3_s{i}", [128, 1024], F32))
                         for i in range(2)]
                cx_ps = [e3(nc.psum_tensor(f"p3_x{i}", [128, 512], F32))
                         for i in range(2)]
                tr3_ps = e3(nc.psum_tensor("p3_t", [128, 1024], BF16))

                p3 = Plan()
                start_phase(p3, bar2)
                pe_sT = {}
                pe_av = {}
                pe_tr = {}
                se_exp = {}
                ve_mask = {}
                ve_ctxn = {}
                ve_trev3 = {}
                wb_ctx = {}
                gctr = 0
                hi = 0
                for j in range(NQT):
                    nch = 2 * j + 2
                    ngr = (nch + 7) // 8
                    w = [(WB, wb_q_all)]
                    if j >= 1:
                        w.append((PE, pe_sT.get((j - 1, 31), 0)))
                    p3.add("sync",
                           (lambda eng, j=j: eng.dma_start(
                               bass.AP(q_sb, 0,
                                       [[H_ * 128, 128], [128, KC],
                                        [1, 128]]),
                               bass.AP(q_scr.tensor, j * 128,
                                       [[OWN, 128], [128 * OWN, KC],
                                        [1, 128]]))),
                           waits=w, inc=(LS, 16))
                    ls_q = LS.v
                    for h in range(H_):
                        kv = h // 4
                        base_g = gctr
                        for gr in range(ngr):
                            c0, c1 = gr * 8, min(gr * 8 + 8, nch)
                            for c in range(c0, c1):
                                waits = []
                                if c == c0:
                                    waits = [(LS, ls_q)]
                                    if gctr >= 2:
                                        waits.append((SE,
                                                      se_exp[gctr - 2]))
                                inc = (PE, 1) if c == c1 - 1 else None
                                p3.add("tensor",
                                       (lambda eng, kv=kv, c=c, c0=c0, h=h,
                                        gctr=gctr: eng.matmul(
                                            sT_ps[gctr % 2][
                                                :, (c - c0) * 128:
                                                (c - c0) * 128 + 128],
                                            K_sb[:, kv * T_ + c * 128:
                                                 kv * T_ + c * 128 + 128],
                                            q_sb[:, h * 128:h * 128 + 128],
                                            start=True, stop=True)),
                                       waits=waits, inc=inc)
                            pe_sT[(j, h)] = PE.v
                            w2 = [(PE, PE.v)]
                            if hi >= 2:
                                w2.append((PE, pe_av[hi - 2]))
                            p3.add("scalar",
                                   (lambda eng, h=h, c0=c0, c1=c1,
                                    gctr=gctr, hi=hi: eng.activation(
                                        pT[hi % 2][:, c0 * 128:c1 * 128],
                                        sT_ps[gctr % 2][
                                            :, 0:(c1 - c0) * 128],
                                        AF.Exp,
                                        bias=negb_sb[:, h:h + 1],
                                        scale=1.0)),
                                   waits=w2, inc=(SE, 1))
                            se_exp[gctr] = SE.v
                            gctr += 1
                        p3.add("vector",
                               (lambda eng, j=j, hi=hi: eng.tensor_mul(
                                   pT[hi % 2][:, 2 * j * 128:
                                              2 * j * 128 + 256],
                                   pT[hi % 2][:, 2 * j * 128:
                                              2 * j * 128 + 256],
                                   mask_sb[:])),
                               waits=[(SE, se_exp[gctr - 1])], inc=(VE, 1))
                        ve_mask[hi] = VE.v
                        for c in range(nch):
                            waits = []
                            if c == 0:
                                waits = [(SE, se_exp[base_g])]
                                if hi >= 2:
                                    waits.append((VE, ve_ctxn[hi - 2]))
                            if c == 2 * j:
                                waits.append((VE, ve_mask[hi]))
                            elif c % 8 == 0 and c > 0:
                                waits.append((SE, se_exp[base_g + c // 8]))
                            inc = (PE, 1) if c == nch - 1 else None
                            p3.add("tensor",
                                   (lambda eng, c=c, kv=kv, hi=hi, nch=nch:
                                    eng.matmul(
                                        cx_ps[hi % 2][:, 0:129],
                                        pT[hi % 2][:, c * 128:c * 128 + 128],
                                        V_sb[:, c * VW + kv * 129:
                                             c * VW + kv * 129 + 129],
                                        start=(c == 0),
                                        stop=(c == nch - 1))),
                                   waits=waits, inc=inc)
                        pe_av[hi] = PE.v
                        p3.add("vector",
                               (lambda eng, hi=hi: eng.reciprocal(
                                   rcol[:], cx_ps[hi % 2][:, 128:129])),
                               waits=[(PE, pe_av[hi])], inc=(VE, 1))
                        ve_rc = VE.v
                        p3.add("vector",
                               (lambda eng, hi=hi: eng.tensor_scalar(
                                   ctm[hi % 2][:], cx_ps[hi % 2][:, 0:128],
                                   rcol[:, 0:1], None, ALU.mult)),
                               waits=[(VE, ve_rc)], inc=(VE, 1))
                        ve_ctxn[hi] = VE.v
                        wtr = [(VE, ve_ctxn[hi])]
                        if h % 8 == 0:
                            wtr.append((VE, ve_trev3.get(
                                (j, h // 8 - 1) if h else (j - 1, 3), 0)))
                        p3.add("tensor",
                               (lambda eng, hi=hi, h=h: eng.transpose(
                                   tr3_ps[:, (h % 8) * 128:
                                          (h % 8) * 128 + 128],
                                   ctm[hi % 2][:], ident_bf[:])),
                               waits=wtr, inc=(PE, 1))
                        pe_tr[hi] = PE.v
                        if h % 8 == 7:
                            w3 = [(PE, pe_tr[hi])]
                            if h == 7:
                                w3.append((WB, wb_ctx.get(j - 1, 0)))
                            p3.add("vector",
                                   (lambda eng, h=h: eng.tensor_copy(
                                       cfm[:, (h - 7) * 128:
                                           (h + 1) * 128],
                                       tr3_ps[:, 0:1024])),
                                   waits=w3, inc=(VE, 1))
                            ve_trev3[(j, h // 8)] = VE.v
                        hi += 1
                    for cg in range(0, KC, 8):
                        p3.add("sync",
                               (lambda eng, j=j, cg=cg: eng.dma_start(
                                   bass.AP(ctx_scr.tensor,
                                           cg * 128 * OWN + j * 128,
                                           [[OWN, 128], [128 * OWN, 8],
                                            [1, 128]]),
                                   bass.AP(cfm, cg * 128,
                                           [[H_ * 128, 128], [128, 8],
                                            [1, 128]]))),
                               waits=[(VE, ve_trev3[(j, 3)])],
                               inc=(WB, 16))
                    wb_ctx[j] = WB.v
                    emit_dq(p3, dqb3, 44 if DBG["nodq1"] else 28)
                wb_ctx_all = WB.v
                if DEBUG_TAPS[0]:
                    p3.add("sync",
                           (lambda eng: eng.dma_start(ptap, pT[1][:])),
                           waits=[(PE, PE.v), (SE, SE.v), (VE, VE.v)],
                           inc=(WB, 16))
                    p3.add("sync", None, waits=[(WB, WB.v)])
                bar3 = barrier(p3)
                emit_block(nc, p3)

        with ExitStack() as s4:
            e4 = s4.enter_context
            cfull = e4(nc.sbuf_tensor("p4_c", [128, KC * 1024], BF16))
            wob = [e4(nc.sbuf_tensor(f"p4_w{i}", [128, KC * 128], BF16))
                   for i in range(2)]
            xres = [e4(nc.sbuf_tensor(f"p4_xr{i}", [128, 1024], F32))
                    for i in range(2)]
            presq = [e4(nc.sbuf_tensor(f"p4_pr{i}", [128, 2048], F32))
                     for i in range(2)]
            nrow = e4(nc.sbuf_tensor("p4_nr", [1, 1024], F32))
            rrow = e4(nc.sbuf_tensor("p4_rr", [1, 1024], F32))
            srow_t = e4(nc.sbuf_tensor("p4_sr", [1, 1024], F32))
            nmb = e4(nc.sbuf_tensor("p4_nm", [128, 1024], F32))
            rsb = e4(nc.sbuf_tensor("p4_rs", [128, 1024], F32))
            pl = [e4(nc.sbuf_tensor(f"p4_pl{i}", [128, 1024], F32))
                  for i in range(2)]
            h1c = [e4(nc.sbuf_tensor(f"p4_h{i}", [128, 1024], BF16))
                   for i in range(2)]
            dqb4 = mk_dq_bufs(s4, "p4")
            o_ps = e4(nc.psum_tensor("p4_o", [128, 1024], F32))
            st4_ps = e4(nc.psum_tensor("p4_s", [1, 2048], F32))
            bc_ps = e4(nc.psum_tensor("p4_b", [128, 512], F32))

            p4 = Plan()
            start_phase(p4, bar3)
            for en in ENGINES:
                p4.add(en, None, waits=[(WB, wb_ctx_all)])
            pe_o = {}
            ve_pre = {}
            pe_st4 = {}
            wb_pre = {}
            ve_fin = 0
            for th in range(2):
                for cg in range(0, KC, 8):
                    w = []
                    if th == 1 and cg == 0:
                        w = [(PE, pe_o[KC - 1])]
                    p4.add("sync",
                           (lambda eng, th=th, cg=cg: eng.dma_start(
                               bass.AP(cfull, cg * 1024,
                                       [[KC * 1024, 128], [1024, 8],
                                        [1, 1024]]),
                               bass.AP(ctx_scr.tensor,
                                       cg * 128 * OWN + th * 1024,
                                       [[OWN, 128], [128 * OWN, 8],
                                        [1, 1024]]))),
                           waits=w, inc=(LS, 16))
                ld_cf = LS.v
                for ep in range(KC):
                    it = th * KC + ep
                    w = []
                    if it >= 2:
                        w.append((PE, pe_o[it - 2]))
                    for lo in (0, 16):
                        p4.add("gpsimd",
                               (lambda eng, ep=ep, it=it, lo=lo:
                                eng.dma_start(
                                    bass.AP(wob[it % 2], lo * 128,
                                            [[KC * 128, 128], [128, 16],
                                             [1, 128]]),
                                    bass.AP(woT.tensor,
                                            lo * 128 * E_ + ep * 128,
                                            [[E_, 128], [128 * E_, 16],
                                             [1, 128]]))),
                               waits=(list(w) if lo == 0 else []),
                               inc=(LD, 16))
                    ld_wo = LD.v
                    wxr = list(w)
                    if it >= 2:
                        wxr.append((VE, ve_pre[it - 2]))
                    p4.add("sync",
                           (lambda eng, ep=ep, th=th, it=it: eng.dma_start(
                               xres[it % 2][:],
                               xTo[ep * 128:(ep + 1) * 128,
                                   th * 1024:(th + 1) * 1024])),
                           waits=wxr, inc=(LS, 16))
                    ld_xr = LS.v
                    for ec in range(KC):
                        waits = []
                        if ec == 0:
                            waits = [(LD, ld_wo), (LS, max(ld_cf, ld_xr))]
                            if it >= 1:
                                waits.append((VE, ve_pre[it - 1]))
                        for ts in range(2):
                            inc = ((PE, 1)
                                   if ec == KC - 1 and ts == 1 else None)
                            p4.add("tensor",
                                   (lambda eng, ec=ec, ts=ts, th=th, it=it:
                                    eng.matmul(
                                        o_ps[:, ts * 512:ts * 512 + 512],
                                        wob[it % 2][:, ec * 128:
                                                    ec * 128 + 128],
                                        cfull[:, ec * 1024 + ts * 512:
                                              ec * 1024 + ts * 512 + 512],
                                        start=(ec == 0),
                                        stop=(ec == KC - 1))),
                                   waits=waits, inc=inc)
                    pe_o[it] = PE.v
                    w2 = [(PE, pe_o[it])]
                    if it >= 2:
                        w2 += [(WB, wb_pre[it - 2]),
                               (PE, pe_st4.get(it - 2, 0))]
                    for ts in range(2):
                        p4.add("vector",
                               (lambda eng, ts=ts, it=it: eng.tensor_add(
                                   presq[it % 2][:, ts * 1024:
                                                 ts * 1024 + 512],
                                   o_ps[:, ts * 512:ts * 512 + 512],
                                   xres[it % 2][:, ts * 512:
                                                ts * 512 + 512])),
                               waits=(w2 if ts == 0 else []))
                        p4.add("vector",
                               (lambda eng, ts=ts, it=it: eng.tensor_mul(
                                   presq[it % 2][:, ts * 1024 + 512:
                                                 ts * 1024 + 1024],
                                   presq[it % 2][:, ts * 1024:
                                                 ts * 1024 + 512],
                                   presq[it % 2][:, ts * 1024:
                                                 ts * 1024 + 512])),
                               inc=((VE, 1) if ts == 1 else None))
                    ve_pre[it] = VE.v
                    for q4 in range(4):
                        w3 = [(VE, ve_pre[it])]
                        if ep == 0 and q4 == 0:
                            w3.append((VE, ve_fin))
                        p4.add("tensor",
                               (lambda eng, q4=q4, ep=ep, it=it: eng.matmul(
                                   st4_ps[0:1, q4 * 512:q4 * 512 + 512],
                                   ones_col_f[:],
                                   presq[it % 2][:, q4 * 512:
                                                 q4 * 512 + 512],
                                   start=(ep == 0), stop=(ep == KC - 1),
                                   skip_group_check=True)),
                               waits=w3,
                               inc=((PE, 1) if q4 == 3 else None))
                    pe_st4[it] = PE.v
                    for ts in range(2):
                        p4.add("sync",
                               (lambda eng, ep=ep, th=th, ts=ts, it=it:
                                eng.dma_start(
                                    pre_scr[ep * 128:(ep + 1) * 128,
                                            th * 1024 + ts * 512:
                                            th * 1024 + ts * 512 + 512],
                                    presq[it % 2][:, ts * 1024:
                                                  ts * 1024 + 512])),
                               waits=[(VE, ve_pre[it])], inc=(WB, 16))
                    wb_pre[it] = WB.v
                    emit_dq(p4, dqb4, 3)
                pe_last = pe_st4[th * KC + KC - 1]
                for ts in range(2):
                    p4.add("vector",
                           (lambda eng, ts=ts: eng.tensor_scalar(
                               nrow[0:1, ts * 512:ts * 512 + 512],
                               st4_ps[0:1, ts * 1024:ts * 1024 + 512],
                               -1.0 / 4096.0, None, ALU.mult)),
                           waits=[(PE, pe_last)])
                    p4.add("vector",
                           (lambda eng, ts=ts: eng.tensor_scalar(
                               srow_t[0:1, ts * 512:ts * 512 + 512],
                               st4_ps[0:1, ts * 1024 + 512:
                                      ts * 1024 + 1024],
                               1.0 / 4096.0, None, ALU.mult)))
                    p4.add("vector",
                           (lambda eng, ts=ts: eng.tensor_mul(
                               rrow[0:1, ts * 512:ts * 512 + 512],
                               nrow[0:1, ts * 512:ts * 512 + 512],
                               nrow[0:1, ts * 512:ts * 512 + 512])))
                    p4.add("vector",
                           (lambda eng, ts=ts: eng.tensor_sub(
                               rrow[0:1, ts * 512:ts * 512 + 512],
                               srow_t[0:1, ts * 512:ts * 512 + 512],
                               rrow[0:1, ts * 512:ts * 512 + 512])))
                    p4.add("vector",
                           (lambda eng, ts=ts: eng.tensor_scalar_add(
                               rrow[0:1, ts * 512:ts * 512 + 512],
                               rrow[0:1, ts * 512:ts * 512 + 512],
                               1.0e-5)),
                           inc=(VE, 1))
                    ve_var = VE.v
                    p4.add("scalar",
                           (lambda eng, ts=ts: eng.sqrt(
                               rrow[0:1, ts * 512:ts * 512 + 512],
                               rrow[0:1, ts * 512:ts * 512 + 512])),
                           waits=[(VE, ve_var)], inc=(SE, 1))
                    se_s = SE.v
                    p4.add("vector",
                           (lambda eng, ts=ts: eng.reciprocal(
                               rrow[0:1, ts * 512:ts * 512 + 512],
                               rrow[0:1, ts * 512:ts * 512 + 512])),
                           waits=[(SE, se_s)], inc=(VE, 1))
                ve_rows = VE.v
                ve_fin = ve_rows
                ve_bcm = 0
                for src_row, dstb in ((nrow, nmb), (rrow, rsb)):
                    for ts in range(2):
                        p4.add("tensor",
                               (lambda eng, src_row=src_row, ts=ts:
                                eng.matmul(
                                    bc_ps[:], ones_row[:],
                                    src_row[0:1, ts * 512:ts * 512 + 512],
                                    start=True, stop=True)),
                               waits=[(VE, ve_rows), (VE, ve_bcm)],
                               inc=(PE, 1))
                        pe_bc = PE.v
                        p4.add("vector",
                               (lambda eng, dstb=dstb, ts=ts:
                                eng.tensor_copy(
                                    dstb[:, ts * 512:ts * 512 + 512],
                                    bc_ps[:])),
                               waits=[(PE, pe_bc)], inc=(VE, 1))
                        ve_bcm = VE.v
                ve_bc = VE.v
                ve_nrm = {}
                wb_h1 = {}
                for ep in range(KC):
                    it = th * KC + ep
                    w = [(WB, wb_pre[it])]
                    if ep >= 2:
                        w.append((VE, ve_nrm[ep - 2]))
                    p4.add("sync",
                           (lambda eng, ep=ep, th=th, it=it: eng.dma_start(
                               pl[it % 2][:],
                               pre_scr[ep * 128:(ep + 1) * 128,
                                       th * 1024:(th + 1) * 1024])),
                           waits=w, inc=(LS, 16))
                    ld_pl = LS.v
                    w2 = [(LS, ld_pl), (VE, ve_bc)]
                    if ep >= 2:
                        w2.append((WB, wb_h1[ep - 2]))
                    p4.add("vector",
                           (lambda eng, it=it: eng.tensor_add(
                               h1c[it % 2].bitcast(F32)[:, 0:1024]
                               if False else pl[it % 2][:],
                               pl[it % 2][:], nmb[:])),
                           waits=w2)
                    p4.add("vector",
                           (lambda eng, it=it: eng.tensor_mul(
                               h1c[it % 2][:], pl[it % 2][:], rsb[:])),
                           inc=(VE, 1))
                    ve_nrm[ep] = VE.v
                    p4.add("sync",
                           (lambda eng, ep=ep, th=th, it=it: eng.dma_start(
                               h1_scr[ep * 128:(ep + 1) * 128,
                                      th * 1024:(th + 1) * 1024],
                               h1c[it % 2][:])),
                           waits=[(VE, ve_nrm[ep])], inc=(WB, 16))
                    wb_h1[ep] = WB.v
            emit_dq(p4, dqb4, len(DQ_ITEMS))
            wb_h1_all = WB.v
            bar4 = barrier(p4)
            emit_block(nc, p4)

        with ExitStack() as s5:
            e5 = s5.enter_context
            h_sb = e5(nc.sbuf_tensor("p5_h", [128, NI * 512], BF16))
            h1p = e5(nc.sbuf_tensor("p5_h1", [128, KC * 512], BF16))
            AW = 22272
            warena = e5(nc.sbuf_tensor("p5_wa", [128, AW], BF16))
            wgb = [warena[:, 0:4224], warena[:, 4224:8448]]
            wub = [warena[:, 8448:12672], warena[:, 12672:16896]]
            wdb = [warena[:, 0:11136], warena[:, 11136:22272]]
            gsum_sb = e5(nc.sbuf_tensor("p5_gs", [32, 512], BF16))
            gsumh_sb = e5(nc.sbuf_tensor("p5_gh", [NI, 512], BF16))
            silu_sb = e5(nc.sbuf_tensor("p5_si", [128, 512], BF16))
            presq5 = [e5(nc.sbuf_tensor(f"p5_pr{i}", [128, 1024], F32))
                      for i in range(2)]
            nrow5 = e5(nc.sbuf_tensor("p5_nr", [1, 512], F32))
            rrow5 = e5(nc.sbuf_tensor("p5_rr", [1, 512], F32))
            srow5 = e5(nc.sbuf_tensor("p5_sr", [1, 512], F32))
            nmb5 = e5(nc.sbuf_tensor("p5_nm", [128, 512], F32))
            rsb5 = e5(nc.sbuf_tensor("p5_rs", [128, 512], F32))
            pl5 = [e5(nc.sbuf_tensor(f"p5_pl{i}", [128, 512], F32))
                   for i in range(2)]
            nrm5 = [e5(nc.sbuf_tensor(f"p5_no{i}", [128, 512], F32))
                    for i in range(2)]
            stg = [e5(nc.sbuf_tensor(f"p5_sg{i}", [128, 512], F32))
                   for i in range(2)]
            gps = e5(nc.psum_tensor("p5_g", [128, 512], F32))
            ups = e5(nc.psum_tensor("p5_u", [128, 512], F32))
            dps = [e5(nc.psum_tensor(f"p5_d{i}", [128, 512], F32))
                   for i in range(2)]
            st5_ps = e5(nc.psum_tensor("p5_s", [1, 1024], F32))
            sh_ps = e5(nc.psum_tensor("p5_p", [128, 512], F32))

            p5 = Plan()
            start_phase(p5, bar4)
            for en in ("sync", "scalar", "gpsimd"):
                p5.add(en, None, waits=[(DQW, DQW_ALL), (WB, wb_h1_all)])
            pe_gate = {}
            pe_up = {}
            se_silu = {}
            ve_h = {}
            pe_dps = {}
            ve_pre5 = {}
            pe_st5 = {}
            wb_pre5 = {}
            outc_d = {}
            pe_pass_end = {}
            ve_pass_end = {}
            for pp in range(4):
                w = [(WB, wb_h1_all)]
                if pp >= 1:
                    w += [(PE, pe_pass_end[pp - 1]),
                          (VE, ve_pass_end[pp - 1])]
                for cg in range(0, KC, 8):
                    p5.add("sync",
                           (lambda eng, pp=pp, cg=cg: eng.dma_start(
                               bass.AP(h1p, cg * 512,
                                       [[KC * 512, 128], [512, 8],
                                        [1, 512]]),
                               bass.AP(h1_scr.tensor,
                                       cg * 128 * OWN + pp * 512,
                                       [[OWN, 128], [128 * OWN, 8],
                                        [1, 512]]))),
                           waits=(list(w) if cg == 0 else []),
                           inc=(LS, 16))
                ld_h1 = LS.v
                for kc in range(KC):
                    waits = [(LS, ld_h1)] if kc == 0 else []
                    if kc == 0 and pp >= 1:
                        waits.append((VE, ve_pass_end[pp - 1]))
                    p5.add("tensor",
                           (lambda eng, kc=kc: eng.matmul(
                               sh_ps[0:32, :],
                               ind_gu[:, 31 - kc:63 - kc],
                               h1p[:, kc * 512:kc * 512 + 512],
                               start=(kc == 0), stop=(kc == KC - 1))),
                           waits=waits,
                           inc=((PE, 1) if kc == KC - 1 else None))
                pe_gs = PE.v
                p5.add("vector",
                       (lambda eng: eng.tensor_copy(gsum_sb[:],
                                                    sh_ps[0:32, :])),
                       waits=[(PE, pe_gs)], inc=(VE, 1))
                ve_gs = VE.v
                for nc_ in range(NI):
                    it = pp * NI + nc_
                    wld = []
                    if it >= 2:
                        wld.append((PE, pe_gate[it - 2]))
                    if nc_ <= 1:
                        wld.append((PE, pe_dps.get(pp * KC - 1, 0)))
                    for lo, nch_ in ((0, 16), (16, 17)):
                        p5.add("sync",
                               (lambda eng, nc_=nc_, it=it, lo=lo,
                                nch_=nch_: eng.dma_start(
                                    bass.AP(wgb[it % 2].tensor,
                                            wgb[it % 2].offset + lo * 128,
                                            [[AW, 128], [128, nch_],
                                             [1, 128]]),
                                    bass.AP(wg_scr.tensor,
                                            lo * 128 * FFN + nc_ * 128,
                                            [[FFN, 128], [128 * FFN, nch_],
                                             [1, 128]]))),
                               waits=(list(wld) if lo == 0 else []),
                               inc=(LS, 16))
                    ld_wg = LS.v
                    wld2 = []
                    if it >= 2:
                        wld2.append((PE, pe_up[it - 2]))
                    if nc_ <= 1:
                        wld2.append((PE, pe_dps.get(pp * KC - 1, 0)))
                    for lo, nch_ in ((0, 16), (16, 17)):
                        p5.add("scalar",
                               (lambda eng, nc_=nc_, it=it, lo=lo,
                                nch_=nch_: eng.dma_start(
                                    bass.AP(wub[it % 2].tensor,
                                            wub[it % 2].offset + lo * 128,
                                            [[AW, 128], [128, nch_],
                                             [1, 128]]),
                                    bass.AP(wu_scr.tensor,
                                            lo * 128 * FFN + nc_ * 128,
                                            [[FFN, 128], [128 * FFN, nch_],
                                             [1, 128]]))),
                               waits=(list(wld2) if lo == 0 else []),
                               inc=(LA, 16))
                    ld_wu = LA.v
                    for kc in range(KC + 1):
                        waits = []
                        if kc == 0:
                            waits = [(LS, ld_wg), (VE, ve_gs)]
                            if it >= 1:
                                waits.append((SE, se_silu[it - 1]))
                        if kc < KC:
                            p5.add("tensor",
                                   (lambda eng, kc=kc, it=it: eng.matmul(
                                       gps[:],
                                       wgb[it % 2][:, kc * 128:
                                                   kc * 128 + 128],
                                       h1p[:, kc * 512:kc * 512 + 512],
                                       start=(kc == 0), stop=False)),
                                   waits=waits)
                        else:
                            p5.add("tensor",
                                   (lambda eng, it=it: eng.matmul(
                                       gps[:],
                                       wgb[it % 2][0:32,
                                                   KC * 128:KC * 128 + 128],
                                       gsum_sb[0:32, :],
                                       start=False, stop=True)),
                                   waits=waits, inc=(PE, 1))
                    pe_gate[it] = PE.v
                    for kc in range(KC + 1):
                        waits = []
                        if kc == 0:
                            waits = [(LA, ld_wu)]
                            if it >= 1:
                                waits.append((VE, ve_h[it - 1]))
                        if kc < KC:
                            p5.add("tensor",
                                   (lambda eng, kc=kc, it=it: eng.matmul(
                                       ups[:],
                                       wub[it % 2][:, kc * 128:
                                                   kc * 128 + 128],
                                       h1p[:, kc * 512:kc * 512 + 512],
                                       start=(kc == 0), stop=False)),
                                   waits=waits)
                        else:
                            p5.add("tensor",
                                   (lambda eng, it=it: eng.matmul(
                                       ups[:],
                                       wub[it % 2][0:32,
                                                   KC * 128:KC * 128 + 128],
                                       gsum_sb[0:32, :],
                                       start=False, stop=True)),
                                   waits=waits, inc=(PE, 1))
                    pe_up[it] = PE.v
                    wsl = [(PE, pe_gate[it])]
                    if nc_ >= 1:
                        wsl.append((VE, ve_h[it - 1]))
                    p5.add("scalar",
                           (lambda eng: eng.activation(silu_sb[:], gps[:],
                                                       AF.Silu)),
                           waits=wsl, inc=(SE, 1))
                    se_silu[it] = SE.v
                    p5.add("vector",
                           (lambda eng, nc_=nc_: eng.tensor_mul(
                               h_sb[:, nc_ * 512:nc_ * 512 + 512],
                               silu_sb[:], ups[:])),
                           waits=[(SE, se_silu[it]), (PE, pe_up[it])],
                           inc=(VE, 1))
                    ve_h[it] = VE.v
                for gc in range(NI):
                    p5.add("tensor",
                           (lambda eng, gc=gc: eng.matmul(
                               sh_ps[0:NI, :],
                               ind_d[:, 85 - gc:171 - gc],
                               h_sb[:, gc * 512:gc * 512 + 512],
                               start=(gc == 0), stop=(gc == NI - 1))),
                           waits=([(VE, ve_h[pp * NI + NI - 1])]
                                  if gc == 0 else []),
                           inc=((PE, 1) if gc == NI - 1 else None))
                pe_gh = PE.v
                p5.add("vector",
                       (lambda eng: eng.tensor_copy(gsumh_sb[:],
                                                    sh_ps[0:NI, :])),
                       waits=[(PE, pe_gh)], inc=(VE, 1))
                ve_gh = VE.v
                for d in range(KC):
                    it = pp * KC + d
                    wld = []
                    if d >= 2:
                        wld.append((PE, pe_dps[it - 2]))
                    if d <= 1:
                        wld += [(PE, pe_gate[pp * NI + NI - 1]),
                                (PE, pe_up[pp * NI + NI - 1])]
                    nlo = list(range(0, NI + 1, 15))
                    for li, lo in enumerate(nlo):
                        hi_ = min(lo + 15, NI + 1)
                        p5.add("gpsimd",
                               (lambda eng, d=d, it=it, lo=lo, hi_=hi_:
                                eng.dma_start(
                                    bass.AP(wdb[it % 2].tensor,
                                            wdb[it % 2].offset + lo * 128,
                                            [[AW, 128],
                                             [128, hi_ - lo],
                                             [1, 128]]),
                                    bass.AP(wd_scr.tensor,
                                            lo * 128 * E_ + d * 128,
                                            [[E_, 128],
                                             [128 * E_, hi_ - lo],
                                             [1, 128]]))),
                               waits=(list(wld) if li == 0 else []),
                               inc=(LD, 16))
                    ld_wd = LD.v
                    for gc in range(NI + 1):
                        waits = []
                        if gc == 0:
                            waits = [(LD, ld_wd), (VE, ve_gh)]
                            if d >= 2:
                                waits.append((VE, ve_pre5[it - 2]))
                        if gc < NI:
                            p5.add("tensor",
                                   (lambda eng, gc=gc, it=it: eng.matmul(
                                       dps[it % 2][:],
                                       wdb[it % 2][:, gc * 128:
                                                   gc * 128 + 128],
                                       h_sb[:, gc * 512:gc * 512 + 512],
                                       start=(gc == 0), stop=False)),
                                   waits=waits)
                        else:
                            p5.add("tensor",
                                   (lambda eng, it=it: eng.matmul(
                                       dps[it % 2][:],
                                       wdb[it % 2][0:NI,
                                                   NI * 128:NI * 128 + 128],
                                       gsumh_sb[0:NI, :],
                                       start=False, stop=True)),
                                   waits=waits, inc=(PE, 1))
                    pe_dps[it] = PE.v
                    w2 = [(PE, pe_dps[it])]
                    if d >= 2:
                        w2 += [(WB, wb_pre5[it - 2]),
                               (PE, pe_st5.get(it - 2, 0))]
                    p5.add("vector",
                           (lambda eng, d=d, it=it: eng.tensor_add(
                               presq5[it % 2][:, 0:512], dps[it % 2][:],
                               h1p[:, d * 512:d * 512 + 512])),
                           waits=w2)
                    p5.add("vector",
                           (lambda eng, it=it: eng.tensor_mul(
                               presq5[it % 2][:, 512:1024],
                               presq5[it % 2][:, 0:512],
                               presq5[it % 2][:, 0:512])),
                           inc=(VE, 1))
                    ve_pre5[it] = VE.v
                    for q2_ in range(2):
                        w3 = [(VE, ve_pre5[it])]
                        if d == 0 and q2_ == 0 and pp >= 1:
                            w3.append((VE, ve_pass_end[pp - 1]))
                        p5.add("tensor",
                               (lambda eng, q2_=q2_, d=d, it=it:
                                eng.matmul(
                                    st5_ps[0:1, q2_ * 512:q2_ * 512 + 512],
                                    ones_col_f[:],
                                    presq5[it % 2][:, q2_ * 512:
                                                   q2_ * 512 + 512],
                                    start=(d == 0), stop=(d == KC - 1),
                                    skip_group_check=True)),
                               waits=w3,
                               inc=((PE, 1) if q2_ == 1 else None))
                    pe_st5[it] = PE.v
                    p5.add("sync",
                           (lambda eng, d=d, pp=pp, it=it: eng.dma_start(
                               pre_scr[d * 128:(d + 1) * 128,
                                       pp * 512:pp * 512 + 512],
                               presq5[it % 2][:, 0:512])),
                           waits=[(VE, ve_pre5[it])], inc=(WB, 16))
                    wb_pre5[it] = WB.v
                pe_last5 = pe_st5[pp * KC + KC - 1]
                p5.add("vector",
                       (lambda eng: eng.tensor_scalar(
                           nrow5[0:1, :], st5_ps[0:1, 0:512],
                           -1.0 / 4096.0, None, ALU.mult)),
                       waits=[(PE, pe_last5)])
                p5.add("vector",
                       (lambda eng: eng.tensor_scalar(
                           srow5[0:1, :], st5_ps[0:1, 512:1024],
                           1.0 / 4096.0, None, ALU.mult)))
                p5.add("vector",
                       (lambda eng: eng.tensor_mul(rrow5[0:1, :],
                                                   nrow5[0:1, :],
                                                   nrow5[0:1, :])))
                p5.add("vector",
                       (lambda eng: eng.tensor_sub(rrow5[0:1, :],
                                                   srow5[0:1, :],
                                                   rrow5[0:1, :])))
                p5.add("vector",
                       (lambda eng: eng.tensor_scalar_add(
                           rrow5[0:1, :], rrow5[0:1, :], 1.0e-5)),
                       inc=(VE, 1))
                ve_var5 = VE.v
                p5.add("scalar",
                       (lambda eng: eng.sqrt(rrow5[0:1, :], rrow5[0:1, :])),
                       waits=[(VE, ve_var5)], inc=(SE, 1))
                se_s5 = SE.v
                p5.add("vector",
                       (lambda eng: eng.reciprocal(rrow5[0:1, :],
                                                   rrow5[0:1, :])),
                       waits=[(SE, se_s5)], inc=(VE, 1))
                ve_rows5 = VE.v
                ve_bcm5 = 0
                for src_row, dstb in ((nrow5, nmb5), (rrow5, rsb5)):
                    p5.add("tensor",
                           (lambda eng, src_row=src_row: eng.matmul(
                               sh_ps[:], ones_row[:], src_row[0:1, :],
                               start=True, stop=True)),
                           waits=[(VE, ve_rows5), (VE, ve_bcm5)],
                           inc=(PE, 1))
                    pe_bc5 = PE.v
                    p5.add("vector",
                           (lambda eng, dstb=dstb: eng.tensor_copy(
                               dstb[:], sh_ps[:])),
                           waits=[(PE, pe_bc5)], inc=(VE, 1))
                    ve_bcm5 = VE.v
                ve_bc5 = VE.v
                ve_nrm5 = {}
                ve_stev = {}
                for d in range(KC):
                    it = pp * KC + d
                    w = [(WB, wb_pre5[it]), (VE, ve_bc5)]
                    if d >= 2:
                        w.append((VE, ve_nrm5[d - 2]))
                    p5.add("sync",
                           (lambda eng, d=d, pp=pp, it=it: eng.dma_start(
                               pl5[it % 2][:],
                               pre_scr[d * 128:(d + 1) * 128,
                                       pp * 512:pp * 512 + 512])),
                           waits=w, inc=(LS, 16))
                    ld_p5 = LS.v
                    p5.add("vector",
                           (lambda eng, it=it: eng.tensor_add(
                               nrm5[it % 2][:], pl5[it % 2][:], nmb5[:])),
                           waits=[(LS, ld_p5)])
                    p5.add("vector",
                           (lambda eng, it=it: eng.tensor_mul(
                               nrm5[it % 2][:], nrm5[it % 2][:],
                               rsb5[:])),
                           inc=(VE, 1))
                    ve_nrm5[d] = VE.v
                    wtr = [(VE, ve_nrm5[d])]
                    if d == 0:
                        wtr.append((VE, ve_bc5))
                    for ts in range(4):
                        p5.add("tensor",
                               (lambda eng, ts=ts, it=it: eng.transpose(
                                   sh_ps.bitcast(F32)[:, ts * 128:
                                                      ts * 128 + 128]
                                   if False else
                                   sh_ps[:, ts * 128:ts * 128 + 128],
                                   nrm5[it % 2][:, ts * 128:
                                                ts * 128 + 128],
                                   ident_f[:])),
                               waits=(wtr if ts == 0 else []),
                               inc=((PE, 1) if ts == 3 else None))
                    pe_tr5 = PE.v
                    w4 = [(PE, pe_tr5)]
                    if d >= 2:
                        w4.append((OUTC, outc_d.get(it - 2, 0)))
                    p5.add("vector",
                           (lambda eng, it=it: eng.tensor_copy(
                               stg[it % 2][:], sh_ps[:])),
                           waits=w4, inc=(VE, 1))
                    ve_stev[d] = VE.v
                    for ts in range(4):
                        p5.add("sync",
                               (lambda eng, ts=ts, d=d, pp=pp, it=it:
                                eng.dma_start(
                                    out[(pp * 4 + ts) * 128:
                                        (pp * 4 + ts) * 128 + 128,
                                        d * 128:d * 128 + 128],
                                    stg[it % 2][:, ts * 128:
                                                ts * 128 + 128])),
                               waits=([(VE, ve_stev[d])]
                                      if ts == 0 else []),
                               inc=(OUTC, 16))
                    outc_d[it] = OUTC.v
                pe_pass_end[pp] = PE.v
                ve_pass_end[pp] = VE.v
            p5.add("sync", None, waits=[(OUTC, OUTC.v)])
            emit_block(nc, p5)

    return nc


_NC_CACHE = [None]


def kernel(L0_attn_x, L0_attn_Wq, L0_attn_Wk, L0_attn_Wv, L0_attn_Wo,
           L0_ffn_gate_q, L0_ffn_gate_s, L0_ffn_gate_z,
           L0_ffn_up_q, L0_ffn_up_s, L0_ffn_up_z,
           L0_ffn_down_q, L0_ffn_down_s, L0_ffn_down_z):
    x = np.asarray(L0_attn_x, dtype=np.float32)
    shared = dict(
        wqT=np.ascontiguousarray(np.asarray(L0_attn_Wq, np.float32).T),
        wkT=np.ascontiguousarray(np.asarray(L0_attn_Wk, np.float32).T),
        wvT=np.ascontiguousarray(np.asarray(L0_attn_Wv, np.float32).T),
        woT=np.ascontiguousarray(np.asarray(L0_attn_Wo, np.float32).T),
        gq=np.ascontiguousarray(np.asarray(L0_ffn_gate_q, np.int32)),
        gs=np.ascontiguousarray(np.asarray(L0_ffn_gate_s, np.float32)),
        gz=np.ascontiguousarray(np.asarray(L0_ffn_gate_z, np.float32)),
        uq=np.ascontiguousarray(np.asarray(L0_ffn_up_q, np.int32)),
        us=np.ascontiguousarray(np.asarray(L0_ffn_up_s, np.float32)),
        uz=np.ascontiguousarray(np.asarray(L0_ffn_up_z, np.float32)),
        dq=np.ascontiguousarray(np.asarray(L0_ffn_down_q, np.int32)),
        ds=np.ascontiguousarray(np.asarray(L0_ffn_down_s, np.float32)),
        dz=np.ascontiguousarray(np.asarray(L0_ffn_down_z, np.float32)),
    )
    tabs = [_host_tables(r) for r in range(2)]
    in_maps = []
    for c in range(8):
        b, r = c // 2, c % 2
        xt = np.ascontiguousarray(x[b].T)
        own = ((2 * np.arange(NQT)[:, None] + r) * 128
               + np.arange(128)[None, :]).reshape(-1)
        m = dict(shared)
        m.update(tabs[r])
        m["xT"] = xt
        m["xTown"] = np.ascontiguousarray(xt[:, own])
        in_maps.append(m)
    if _NC_CACHE[0] is None:
        _NC_CACHE[0] = build_nc()
    res = run_bass_kernel_spmd(_NC_CACHE[0], in_maps, core_ids=list(range(8)))
    outf = np.empty((B_, T_, E_), dtype=np.float32)
    for c in range(8):
        b, r = c // 2, c % 2
        o = np.asarray(res.results[c]["out"]).reshape(NQT, 128, E_)
        outf[b].reshape(32, 128, E_)[2 * np.arange(NQT) + r] = o
    return outf


if __name__ == "__main__":
    nc = build_nc()
    print("built:", nc.n_instructions(), "instructions")
